# version 04 (final)
# baseline (speedup 1.0000x reference)
import math
import contextlib
import numpy as np
import ml_dtypes
import concourse.bass as bass
import concourse.mybir as mybir
from concourse.bass_utils import run_bass_kernel_spmd

F32 = mybir.dt.float32
BF16 = mybir.dt.bfloat16
AF = mybir.ActivationFunctionType
ALU = mybir.AluOpType
AX = mybir.AxisListType

ENGS = ("tensor", "vector", "scalar", "sync", "gpsimd")
RS = 0.08838834764831845


def seq_shifts(num_shifts=16, max_len=2048):
    half = (num_shifts - 1 + 1) // 2
    max_s = max(max_len // 2, 2)
    positive = sorted(set(max(1, round(math.exp(math.log(max_s) * i / (half - 1)))) for i in range(half)))
    while len(positive) < half:
        new_vals = []
        for i in range(len(positive) - 1):
            mid = (positive[i] + positive[i + 1]) // 2
            if mid not in positive and mid not in new_vals:
                new_vals.append(mid)
            if len(positive) + len(new_vals) >= half:
                break
        if not new_vals:
            v = positive[-1] + 1
            while len(positive) + len(new_vals) < half and v <= max_s:
                new_vals.append(v)
                v += 1
        positive = sorted(set(positive + new_vals))[:half]
    shifts = [0]
    for s in positive:
        shifts.append(s)
        shifts.append(-s)
    return shifts[:num_shifts]

SHIFTS = seq_shifts()
MAXNEG = max(-s for s in SHIFTS if s < 0)
EXTW = 2048 + MAXNEG + 128


class Buf:
    __slots__ = ("w", "r")
    def __init__(self):
        self.w = None
        self.r = []


class Ctx:
    def __init__(self, nc):
        self.nc = nc
        self.stack = contextlib.ExitStack()
        self.sem = {}
        self.cnt = {e: 0 for e in ENGS}
        self.guar = {e: {s: 0 for s in ENGS} for e in ENGS}
        for e in ENGS:
            self.sem[e] = self.stack.enter_context(nc.semaphore("sem_" + e))

    def _waits(self, eng, deps):
        eo = getattr(self.nc, eng)
        m = {}
        for (e, v) in deps:
            if v > 0:
                m[e] = max(m.get(e, 0), v)
        for e, v in m.items():
            if self.guar[eng].get(e, 0) >= v:
                continue
            eo.wait_ge(self.sem[e], v)
            self.guar[eng][e] = v

    def op(self, eng, fname, *args, ins=(), outs=(), psafe=False, **kw):
        deps = []
        for b in ins:
            if b.w:
                deps.append(b.w)
        for b in outs:
            if b.w:
                deps.append(b.w)
            deps.extend(b.r)
        if psafe and eng == "tensor":
            deps = [d for d in deps if d[0] != "tensor"]
        self._waits(eng, deps)
        eo = getattr(self.nc, eng)
        inst = getattr(eo, fname)(*args, **kw)
        inc = 16 if fname == "dma_start" else 1
        self.cnt[eng] += inc
        inst.then_inc(self.sem[eng], inc)
        tok = (eng, self.cnt[eng])
        for b in outs:
            b.w = tok
            b.r = []
        for b in ins:
            b.r.append(tok)
        return tok

    def finish(self, out_bufs):
        deps = []
        for b in out_bufs:
            if b.w:
                deps.append(b.w)
        self._waits("gpsimd", deps)
        self._waits("vector", deps)


def sbuf(ctx, name, shape, dt):
    t = ctx.stack.enter_context(ctx.nc.sbuf_tensor(name, shape, dt))
    return t, Buf()


def psum(ctx, name, shape, dt):
    t = ctx.stack.enter_context(ctx.nc.psum_tensor(name, shape, dt))
    return t, Buf()


def dram(nc, name, shape, dt, kind):
    return nc.dram_tensor(name, shape, dt, kind=kind).ap()


def emit_layernorm(ctx, dst_ap, dst_b, src_ap, src_b, tmp, width):
    red, red_b = tmp["red"]
    xc, xc_b = tmp["xc"]
    var, var_b = tmp["var"]
    eps, eps_b = tmp["eps"]
    w = width
    ctx.op("vector", "reduce_sum", red[:, 0:1], src_ap, axis=AX.X, ins=[src_b], outs=[red_b])
    ctx.op("scalar", "mul", red[:, 1:2], red[:, 0:1], 1.0 / w, ins=[red_b], outs=[red_b])
    ctx.op("vector", "tensor_scalar_sub", xc[:, 0:w], src_ap, red[:, 1:2], ins=[src_b, red_b], outs=[xc_b])
    ctx.op("vector", "tensor_mul", dst_ap, xc[:, 0:w], xc[:, 0:w], ins=[xc_b], outs=[dst_b])
    ctx.op("vector", "reduce_sum", var[:, 0:1], dst_ap, axis=AX.X, ins=[dst_b], outs=[var_b])
    ctx.op("scalar", "mul", var[:, 3:4], var[:, 0:1], 1.0 / w, ins=[var_b], outs=[var_b])
    ctx.op("scalar", "activation", var[:, 1:2], var[:, 3:4], AF.Ln, bias=eps[:, 0:1],
           ins=[var_b, eps_b], outs=[var_b])
    ctx.op("scalar", "activation", var[:, 2:3], var[:, 1:2], AF.Exp, scale=-0.5, ins=[var_b], outs=[var_b])
    ctx.op("vector", "tensor_scalar_mul", dst_ap, xc[:, 0:w], var[:, 2:3], ins=[xc_b, var_b], outs=[dst_b])


def mk_ln_tmp(ctx, name, width, xc=None, eps=1e-5):
    d = {
        "red": sbuf(ctx, name + "_red", [128, 4], F32),
        "xc": xc if xc is not None else sbuf(ctx, name + "_xc", [128, width], F32),
        "var": sbuf(ctx, name + "_var", [128, 4], F32),
        "eps": sbuf(ctx, name + "_eps", [128, 1], F32),
    }
    ctx.op("vector", "memset", d["eps"][0][:], eps, outs=[d["eps"][1]])
    return d


def emit_tr16(ctx, src_ap, src_b, idb, idb_b, tp, tp_b, tcol, tcol_b, n=16):
    for ic in range(n):
        ctx.op("tensor", "transpose", tp[:], src_ap[:, ic * 128:(ic + 1) * 128], idb[:],
               ins=[src_b, idb_b], outs=[tp_b], psafe=True)
        ctx.op("scalar", "copy", tcol[:, ic, :], tp[:], ins=[tp_b], outs=[tcol_b])


def emit_build_wT(ctx, wT, wT_b, w_dram, w_dram_b, nob, stage, stage_b, sbf, sbf_b,
                  idb, idb_b, tp, tp_b, tcol, tcol_b):
    for ob in range(nob):
        ctx.op("sync", "dma_start", out=stage, in_=w_dram[ob * 128:(ob + 1) * 128, :],
               ins=[w_dram_b], outs=[stage_b])
        ctx.op("scalar", "copy", sbf, stage, ins=[stage_b], outs=[sbf_b])
        emit_tr16(ctx, sbf, sbf_b, idb, idb_b, tp, tp_b, tcol, tcol_b)
        for ic in range(16):
            ctx.op("vector", "tensor_copy", wT[:, ic, ob * 128:(ob + 1) * 128], tcol[:, ic, :],
                   ins=[tcol_b], outs=[wT_b])



def build_s1a(nc):
    ctx = Ctx(nc)
    xb = dram(nc, "xb", [2048, 2048], F32, "ExternalInput")
    Wqh = dram(nc, "Wqh", [1024, 2048], F32, "ExternalInput")
    Wkh = dram(nc, "Wkh", [1024, 2048], F32, "ExternalInput")
    Wvh = dram(nc, "Wvh", [1024, 2048], F32, "ExternalInput")
    gamma_t = dram(nc, "gamma_t", [128, 128], F32, "ExternalInput")
    beta_t = dram(nc, "beta_t", [128, 128], F32, "ExternalInput")
    idb_d = dram(nc, "idb", [128, 128], BF16, "ExternalInput")
    tril_d = dram(nc, "trilT", [128, 128], BF16, "ExternalInput")
    o_out = dram(nc, "o_out", [2048, 1024], F32, "ExternalOutput")
    hsT_s = dram(nc, "hsT_s", [128, 16, 2048], BF16, "Internal")
    kT_s = dram(nc, "kT_s", [8, 128, 2048], BF16, "Internal")
    qT_s = dram(nc, "qT_s", [8, 128, 2048], BF16, "Internal")
    v_s = dram(nc, "v_s", [16, 128, 1024], BF16, "Internal")
    B = {n: Buf() for n in ("xb", "Wqh", "Wkh", "Wvh", "gamma_t", "beta_t", "idb", "tril",
                            "hsT_s", "kT_s", "qT_s", "v_s", "o_out")}

    idb, idb_b = sbuf(ctx, "idb_s", [128, 128], BF16)
    tril, tril_b = sbuf(ctx, "tril_s", [128, 128], BF16)
    gam, gam_b = sbuf(ctx, "gam_s", [128, 128], F32)
    bet, bet_b = sbuf(ctx, "bet_s", [128, 128], F32)
    ctx.op("sync", "dma_start", out=idb[:], in_=idb_d[:, :], outs=[idb_b])
    ctx.op("sync", "dma_start", out=tril[:], in_=tril_d[:, :], outs=[tril_b])
    ctx.op("sync", "dma_start", out=gam[:], in_=gamma_t[:, :], outs=[gam_b])
    ctx.op("sync", "dma_start", out=bet[:], in_=beta_t[:, :], outs=[bet_b])

    xt, xt_b = sbuf(ctx, "xt", [128, 2048], F32)
    ln, ln_b = sbuf(ctx, "ln", [128, 2048], F32)
    lnb, lnb_b = sbuf(ctx, "lnb", [128, 2048], BF16)
    tp, tp_b = psum(ctx, "tp", [128, 128], BF16)
    tcol, tcol_b = sbuf(ctx, "tcol", [128, 16, 128], BF16)
    tmp = mk_ln_tmp(ctx, "lnt", 2048)
    tmph = mk_ln_tmp(ctx, "lnh", 128)

    for ti in range(16):
        ctx.op("sync", "dma_start", out=xt[:], in_=xb[ti * 128:(ti + 1) * 128, :],
               ins=[B["xb"]], outs=[xt_b])
        emit_layernorm(ctx, ln[:], ln_b, xt[:], xt_b, tmp, 2048)
        ctx.op("scalar", "copy", lnb[:], ln[:], ins=[ln_b], outs=[lnb_b])
        emit_tr16(ctx, lnb[:], lnb_b, idb, idb_b, tp, tp_b, tcol, tcol_b)
        ctx.op("gpsimd", "dma_start", out=hsT_s[:, :, ti * 128:(ti + 1) * 128], in_=tcol[:],
               ins=[tcol_b], outs=[B["hsT_s"]])

    wT, wT_b = sbuf(ctx, "wT", [128, 16, 1024], BF16)
    hsblk, hsblk_b = sbuf(ctx, "hsblk", [128, 16, 128], BF16)
    pp, pp_b = psum(ctx, "pp", [128, 1024], F32)
    fm, fm_b = sbuf(ctx, "fm", [128, 1024], F32)
    fb, fb_b = sbuf(ctx, "fb", [128, 1024], BF16)
    for mat, wd, wdb in (("q", Wqh, B["Wqh"]), ("k", Wkh, B["Wkh"]), ("v", Wvh, B["Wvh"])):
        emit_build_wT(ctx, wT, wT_b, wd, wdb, 8, xt[:], xt_b, lnb[:], lnb_b,
                      idb, idb_b, tp, tp_b, tcol, tcol_b)
        for ti in range(16):
            ctx.op("sync", "dma_start", out=hsblk[:], in_=hsT_s[:, :, ti * 128:(ti + 1) * 128],
                   ins=[B["hsT_s"]], outs=[hsblk_b])
            for ic in range(16):
                for ch in range(2):
                    ctx.op("tensor", "matmul", pp[:, ch * 512:(ch + 1) * 512],
                           hsblk[:, ic, :], wT[:, ic, ch * 512:(ch + 1) * 512],
                           start=(ic == 0), stop=(ic == 15),
                           ins=[hsblk_b, wT_b], outs=[pp_b], psafe=True)
            if mat in ("q", "k"):
                for h in range(8):
                    hc = slice(h * 128, (h + 1) * 128)
                    ctx.op("vector", "tensor_mul", fm[:, hc], pp[:, hc], gam[:],
                           ins=[pp_b, gam_b], outs=[fm_b])
                    ctx.op("vector", "tensor_add", fm[:, hc], fm[:, hc], bet[:],
                           ins=[fm_b, bet_b], outs=[fm_b])
                    emit_layernorm(ctx, fm[:, hc], fm_b, fm[:, hc], fm_b, tmph, 128)
                ctx.op("scalar", "copy", fb[:], fm[:], ins=[fm_b], outs=[fb_b])
                dst, dstb = (kT_s, B["kT_s"]) if mat == "k" else (qT_s, B["qT_s"])
                for h in range(8):
                    ctx.op("tensor", "transpose", tp[:], fb[:, h * 128:(h + 1) * 128], idb[:],
                           ins=[fb_b, idb_b], outs=[tp_b], psafe=True)
                    ctx.op("scalar", "copy", tcol[:, 0, :], tp[:], ins=[tp_b], outs=[tcol_b])
                    ctx.op("gpsimd", "dma_start", out=dst[h, :, ti * 128:(ti + 1) * 128],
                           in_=tcol[:, 0, :], ins=[tcol_b], outs=[dstb])
            else:
                ctx.op("scalar", "copy", fb[:], pp[:], ins=[pp_b], outs=[fb_b])
                ctx.op("gpsimd", "dma_start", out=v_s[ti, :, :], in_=fb[:],
                       ins=[fb_b], outs=[B["v_s"]])

    kT_h, kT_h_b = sbuf(ctx, "kT_h", [128, 2048], BF16)
    qT_h, qT_h_b = sbuf(ctx, "qT_h", [128, 2048], BF16)
    v_ext, v_ext_b = sbuf(ctx, "v_ext", [128, 16, 132], BF16)
    s_b, s_b_b = sbuf(ctx, "s_b", [128, 128], BF16)
    ps, ps_b = psum(ctx, "ps", [128, 512], F32)
    po, po_b = psum(ctx, "po", [128, 512], F32)
    o_sb, o_sb_b = sbuf(ctx, "o_sb", [128, 128], F32)
    zz, zz_b = sbuf(ctx, "zz", [128, 2], F32)
    for h in range(8):
        ctx.op("sync", "dma_start", out=kT_h[:], in_=kT_s[h, :, :], ins=[B["kT_s"]], outs=[kT_h_b])
        ctx.op("sync", "dma_start", out=qT_h[:], in_=qT_s[h, :, :], ins=[B["qT_s"]], outs=[qT_h_b])
        ctx.op("vector", "memset", v_ext[:, :, 128:129], 1.0, outs=[v_ext_b])
        for kj in range(16):
            ctx.op("sync", "dma_start", out=v_ext[:, kj, 0:128], in_=v_s[kj, :, h * 128:(h + 1) * 128],
                   ins=[B["v_s"]], outs=[v_ext_b])
        for qt in reversed(range(16)):
            for kj in range(qt + 1):
                ctx.op("tensor", "matmul", ps[:, 0:128], kT_h[:, kj * 128:(kj + 1) * 128],
                       qT_h[:, qt * 128:(qt + 1) * 128], start=True, stop=True,
                       ins=[kT_h_b, qT_h_b], outs=[ps_b])
                ctx.op("scalar", "activation", s_b[:], ps[:, 0:128], AF.Square, scale=RS,
                       ins=[ps_b], outs=[s_b_b])
                if kj == qt:
                    ctx.op("vector", "tensor_mul", s_b[:], s_b[:], tril[:],
                           ins=[s_b_b, tril_b], outs=[s_b_b])
                ctx.op("tensor", "matmul", po[:, 0:129], s_b[:], v_ext[:, kj, 0:129],
                       start=(kj == 0), stop=(kj == qt),
                       ins=[s_b_b, v_ext_b], outs=[po_b])
            ctx.op("vector", "tensor_scalar_add", zz[:, 0:1], po[:, 128:129], 1e-5,
                   ins=[po_b], outs=[zz_b])
            ctx.op("vector", "reciprocal", zz[:, 1:2], zz[:, 0:1], ins=[zz_b], outs=[zz_b])
            ctx.op("vector", "tensor_scalar_mul", o_sb[:], po[:, 0:128], zz[:, 1:2],
                   ins=[po_b, zz_b], outs=[o_sb_b])
            ctx.op("gpsimd", "dma_start", out=o_out[qt * 128:(qt + 1) * 128, h * 128:(h + 1) * 128],
                   in_=o_sb[:], ins=[o_sb_b], outs=[B["o_out"]])
    ctx.finish([B["o_out"]])
    return nc



def build_s1b(nc):
    ctx = Ctx(nc)
    orows = dram(nc, "orows", [1024, 2048], F32, "ExternalInput")
    xown = dram(nc, "xown", [1024, 2048], F32, "ExternalInput")
    Wo = dram(nc, "Wo", [2048, 2048], F32, "ExternalInput")
    idb_d = dram(nc, "idb", [128, 128], BF16, "ExternalInput")
    x1 = dram(nc, "x1", [1024, 2048], F32, "ExternalOutput")
    B = {n: Buf() for n in ("orows", "xown", "Wo", "idb", "x1")}

    idb, idb_b = sbuf(ctx, "idb_s", [128, 128], BF16)
    ctx.op("sync", "dma_start", out=idb[:], in_=idb_d[:, :], outs=[idb_b])
    xt, xt_b = sbuf(ctx, "xt", [128, 2048], F32)
    lnb, lnb_b = sbuf(ctx, "lnb", [128, 2048], BF16)
    tp, tp_b = psum(ctx, "tp", [128, 128], BF16)
    tcol, tcol_b = sbuf(ctx, "tcol", [128, 16, 128], BF16)
    wT, wT_b = sbuf(ctx, "wT", [128, 16, 2048], BF16)
    oT, oT_b = sbuf(ctx, "oT", [128, 16, 1024], BF16)
    pp, pp_b = psum(ctx, "pp", [128, 2048], F32)
    fm, fm_b = sbuf(ctx, "fm", [128, 2048], F32)

    emit_build_wT(ctx, wT, wT_b, Wo, B["Wo"], 16, xt[:], xt_b, lnb[:], lnb_b,
                  idb, idb_b, tp, tp_b, tcol, tcol_b)
    for ti in range(8):
        ctx.op("sync", "dma_start", out=xt[:], in_=orows[ti * 128:(ti + 1) * 128, :],
               ins=[B["orows"]], outs=[xt_b])
        ctx.op("scalar", "copy", lnb[:], xt[:], ins=[xt_b], outs=[lnb_b])
        emit_tr16(ctx, lnb[:], lnb_b, idb, idb_b, tp, tp_b, tcol, tcol_b)
        for ic in range(16):
            ctx.op("vector", "tensor_copy", oT[:, ic, ti * 128:(ti + 1) * 128], tcol[:, ic, :],
                   ins=[tcol_b], outs=[oT_b])
    for ti in range(8):
        for ch in range(4):
            for ic in range(16):
                ctx.op("tensor", "matmul", pp[:, ch * 512:(ch + 1) * 512],
                       oT[:, ic, ti * 128:(ti + 1) * 128], wT[:, ic, ch * 512:(ch + 1) * 512],
                       start=(ic == 0), stop=(ic == 15),
                       ins=[oT_b, wT_b], outs=[pp_b], psafe=True)
        ctx.op("sync", "dma_start", out=xt[:], in_=xown[ti * 128:(ti + 1) * 128, :],
               ins=[B["xown"]], outs=[xt_b])
        ctx.op("vector", "tensor_add", fm[:], pp[:], xt[:], ins=[pp_b, xt_b], outs=[fm_b])
        ctx.op("gpsimd", "dma_start", out=x1[ti * 128:(ti + 1) * 128, :], in_=fm[:],
               ins=[fm_b], outs=[B["x1"]])
    ctx.finish([B["x1"]])
    return nc



def build_s2(nc):
    ctx = Ctx(nc)
    x1 = dram(nc, "x1", [1024, 2048], F32, "ExternalInput")
    gate_w = dram(nc, "gate_w", [128, 16, 8], F32, "ExternalInput")
    w_gate = dram(nc, "w_gate", [8, 2048, 1408], F32, "ExternalInput")
    w_up = dram(nc, "w_up", [8, 2048, 1408], F32, "ExternalInput")
    w_down = dram(nc, "w_down", [8, 1408, 2048], F32, "ExternalInput")
    sw_gate = dram(nc, "sw_gate", [2048, 2816], F32, "ExternalInput")
    sw_up = dram(nc, "sw_up", [2048, 2816], F32, "ExternalInput")
    sw_down = dram(nc, "sw_down", [2816, 2048], F32, "ExternalInput")
    idb_d = dram(nc, "idb", [128, 128], BF16, "ExternalInput")
    x2 = dram(nc, "x2", [1024, 2048], F32, "ExternalOutput")
    dbgc = dram(nc, "dbgc", [2, 128, 32], F32, "ExternalOutput")
    B = {n: Buf() for n in ("x1", "gate_w", "w_gate", "w_up", "w_down", "sw_gate", "sw_up",
                            "sw_down", "idb", "x2", "dbgc")}

    idb, idb_b = sbuf(ctx, "idb_s", [128, 128], BF16)
    ctx.op("sync", "dma_start", out=idb[:], in_=idb_d[:, :], outs=[idb_b])
    gwf, gwf_b = sbuf(ctx, "gwf", [128, 16, 8], F32)
    gw, gw_b = sbuf(ctx, "gw", [128, 16, 8], BF16)
    ctx.op("sync", "dma_start", out=gwf[:], in_=gate_w[:, :, :], ins=[B["gate_w"]], outs=[gwf_b])
    ctx.op("scalar", "copy", gw[:], gwf[:], ins=[gwf_b], outs=[gw_b])

    lnT, lnT_b = sbuf(ctx, "lnT", [128, 16, 512], BF16)
    stage, stage_b = sbuf(ctx, "stage", [128, 2048], F32)
    ln, ln_b = sbuf(ctx, "ln", [128, 2048], F32)
    lnb, lnb_b = sbuf(ctx, "lnb", [128, 2048], BF16)
    scr, scr_b = sbuf(ctx, "scr", [128, 2048], F32)
    tp, tp_b = psum(ctx, "tp", [128, 128], BF16)
    tcol, tcol_b = sbuf(ctx, "tcol", [128, 16, 128], BF16)
    tmp = mk_ln_tmp(ctx, "lnt", 2048, xc=(scr, scr_b))
    acc, acc_b = sbuf(ctx, "acc", [128, 4, 2048], F32)
    comb, comb_b = sbuf(ctx, "comb", [128, 32], F32)
    sc8, sc8_b = sbuf(ctx, "sc8", [128, 8], F32)
    scw, scw_b = sbuf(ctx, "scw", [128, 10], F32)
    pbig, pbig_b = psum(ctx, "pbig", [128, 2048], F32)
    pg, pg_b = psum(ctx, "pg", [128, 512], F32)
    pu, pu_b = psum(ctx, "pu", [128, 512], F32)
    wpool, wpool_b = sbuf(ctx, "wpool", [128, 45056], BF16)
    hpool, hpool_b = sbuf(ctx, "hpool", [128, 11264], BF16)
    sg, sg_b = sbuf(ctx, "sg", [128, 1408], F32)

    for grp in range(2):
        for tl in range(4):
            ti = grp * 4 + tl
            ctx.op("sync", "dma_start", out=stage[:], in_=x1[ti * 128:(ti + 1) * 128, :],
                   ins=[B["x1"]], outs=[stage_b])
            emit_layernorm(ctx, ln[:], ln_b, stage[:], stage_b, tmp, 2048)
            ctx.op("scalar", "copy", lnb[:], ln[:], ins=[ln_b], outs=[lnb_b])
            emit_tr16(ctx, lnb[:], lnb_b, idb, idb_b, tp, tp_b, tcol, tcol_b)
            for ic in range(16):
                ctx.op("vector", "tensor_copy", lnT[:, ic, tl * 128:(tl + 1) * 128], tcol[:, ic, :],
                       ins=[tcol_b], outs=[lnT_b])
            for ic in range(16):
                ctx.op("tensor", "matmul", pg[:, 0:8], lnT[:, ic, tl * 128:(tl + 1) * 128],
                       gw[:, ic, :], start=(ic == 0), stop=(ic == 15),
                       ins=[lnT_b, gw_b], outs=[pg_b], psafe=True)
            ctx.op("vector", "reduce_max", scw[:, 0:1], pg[:, 0:8], axis=AX.X, ins=[pg_b], outs=[scw_b])
            ctx.op("scalar", "mul", scw[:, 1:2], scw[:, 0:1], -1.0, ins=[scw_b], outs=[scw_b])
            ctx.op("scalar", "activation", sc8[:], pg[:, 0:8], AF.Exp, bias=scw[:, 1:2],
                   ins=[pg_b, scw_b], outs=[sc8_b])
            ctx.op("vector", "reduce_sum", scw[:, 2:3], sc8[:], axis=AX.X, ins=[sc8_b], outs=[scw_b])
            ctx.op("vector", "reciprocal", scw[:, 3:4], scw[:, 2:3], ins=[scw_b], outs=[scw_b])
            ctx.op("vector", "tensor_scalar_mul", sc8[:], sc8[:], scw[:, 3:4],
                   ins=[sc8_b, scw_b], outs=[sc8_b])
            ctx.op("vector", "reduce_max", scw[:, 4:5], sc8[:], axis=AX.X, ins=[sc8_b], outs=[scw_b])
            ctx.op("vector", "tensor_scalar", comb[:, tl * 8:tl * 8 + 8], sc8[:], scw[:, 4:5], None,
                   ALU.is_equal, ins=[sc8_b, scw_b], outs=[comb_b])
            ctx.op("vector", "tensor_scalar_mul", comb[:, tl * 8:tl * 8 + 8], comb[:, tl * 8:tl * 8 + 8], -1e9,
                   ins=[comb_b], outs=[comb_b])
            ctx.op("vector", "tensor_add", comb[:, tl * 8:tl * 8 + 8], comb[:, tl * 8:tl * 8 + 8], sc8[:],
                   ins=[comb_b, sc8_b], outs=[comb_b])
            ctx.op("vector", "reduce_max", scw[:, 5:6], comb[:, tl * 8:tl * 8 + 8], axis=AX.X,
                   ins=[comb_b], outs=[scw_b])
            ctx.op("vector", "tensor_scalar", comb[:, tl * 8:tl * 8 + 8], sc8[:], scw[:, 5:6], None,
                   ALU.is_ge, ins=[sc8_b, scw_b], outs=[comb_b])
            ctx.op("vector", "tensor_mul", comb[:, tl * 8:tl * 8 + 8], comb[:, tl * 8:tl * 8 + 8], sc8[:],
                   ins=[comb_b, sc8_b], outs=[comb_b])
            ctx.op("vector", "tensor_copy", acc[:, tl, :], stage[:], ins=[stage_b], outs=[acc_b])
        ctx.op("gpsimd", "dma_start", out=dbgc[grp, :, :], in_=comb[:], ins=[comb_b], outs=[B["dbgc"]])

        ocs = [(0, 512), (512, 1024), (1024, 1536), (1536, 2048), (2048, 2560), (2560, 2816)]
        for oi, (o0, o1) in enumerate(ocs):
            w = o1 - o0
            for ic in range(16):
                ctx.op("sync", "dma_start", out=stage[:, 0:w], in_=sw_gate[ic * 128:(ic + 1) * 128, o0:o1],
                       ins=[B["sw_gate"]], outs=[stage_b])
                ctx.op("scalar", "copy", wpool[:, ic * 512:ic * 512 + w], stage[:, 0:w],
                       ins=[stage_b], outs=[wpool_b])
                ctx.op("sync", "dma_start", out=stage[:, 0:w], in_=sw_up[ic * 128:(ic + 1) * 128, o0:o1],
                       ins=[B["sw_up"]], outs=[stage_b])
                ctx.op("scalar", "copy", wpool[:, 8192 + ic * 512:8192 + ic * 512 + w], stage[:, 0:w],
                       ins=[stage_b], outs=[wpool_b])
            for tl in range(4):
                for ic in range(16):
                    ctx.op("tensor", "matmul", pg[:, 0:w], lnT[:, ic, tl * 128:(tl + 1) * 128],
                           wpool[:, ic * 512:ic * 512 + w], start=(ic == 0), stop=(ic == 15),
                           ins=[lnT_b, wpool_b], outs=[pg_b], psafe=True)
                    ctx.op("tensor", "matmul", pu[:, 0:w], lnT[:, ic, tl * 128:(tl + 1) * 128],
                           wpool[:, 8192 + ic * 512:8192 + ic * 512 + w], start=(ic == 0), stop=(ic == 15),
                           ins=[lnT_b, wpool_b], outs=[pu_b], psafe=True)
                ctx.op("scalar", "activation", sg[:, 0:w], pg[:, 0:w], AF.Silu, ins=[pg_b], outs=[sg_b])
                ctx.op("vector", "tensor_mul", sg[:, 0:w], sg[:, 0:w], pu[:, 0:w],
                       ins=[sg_b, pu_b], outs=[sg_b])
                ctx.op("scalar", "copy", lnb[:, 0:w], sg[:, 0:w], ins=[sg_b], outs=[lnb_b])
                for j in range(w // 128):
                    ctx.op("tensor", "transpose", tp[:], lnb[:, j * 128:(j + 1) * 128], idb[:],
                           ins=[lnb_b, idb_b], outs=[tp_b], psafe=True)
                    ctx.op("scalar", "copy",
                           hpool[:, (oi * 4 + j) * 512 + tl * 128:(oi * 4 + j) * 512 + (tl + 1) * 128],
                           tp[:], ins=[tp_b], outs=[hpool_b])
        for dc in range(4):
            for ob in range(22):
                ctx.op("sync", "dma_start", out=stage[:, 0:512],
                       in_=sw_down[ob * 128:(ob + 1) * 128, dc * 512:(dc + 1) * 512],
                       ins=[B["sw_down"]], outs=[stage_b])
                ctx.op("scalar", "copy", wpool[:, ob * 512:(ob + 1) * 512], stage[:, 0:512],
                       ins=[stage_b], outs=[wpool_b])
            for tl in range(4):
                for ob in range(22):
                    ctx.op("tensor", "matmul", pbig[:, dc * 512:(dc + 1) * 512],
                           hpool[:, ob * 512 + tl * 128:ob * 512 + (tl + 1) * 128],
                           wpool[:, ob * 512:(ob + 1) * 512],
                           start=(ob == 0), stop=(ob == 21),
                           ins=[hpool_b, wpool_b], outs=[pbig_b], psafe=True)
                ctx.op("vector", "tensor_add", acc[:, tl, dc * 512:(dc + 1) * 512],
                       acc[:, tl, dc * 512:(dc + 1) * 512], pbig[:, dc * 512:(dc + 1) * 512],
                       ins=[acc_b, pbig_b], outs=[acc_b])

        for e in range(8):
            for ic in range(16):
                ctx.op("sync", "dma_start", out=stage[:, 0:1408],
                       in_=w_gate[e, ic * 128:(ic + 1) * 128, :], ins=[B["w_gate"]], outs=[stage_b])
                ctx.op("scalar", "copy", wpool[:, ic * 1408:(ic + 1) * 1408], stage[:, 0:1408],
                       ins=[stage_b], outs=[wpool_b])
                ctx.op("sync", "dma_start", out=stage[:, 0:1408],
                       in_=w_up[e, ic * 128:(ic + 1) * 128, :], ins=[B["w_up"]], outs=[stage_b])
                ctx.op("scalar", "copy", wpool[:, 22528 + ic * 1408:22528 + (ic + 1) * 1408],
                       stage[:, 0:1408], ins=[stage_b], outs=[wpool_b])
            for tl in range(4):
                for ic in range(16):
                    st_ap = lnT[:, ic, tl * 128:(tl + 1) * 128]
                    for c5 in range(2):
                        ctx.op("tensor", "matmul", pbig[:, c5 * 512:(c5 + 1) * 512], st_ap,
                               wpool[:, ic * 1408 + c5 * 512:ic * 1408 + (c5 + 1) * 512],
                               start=(ic == 0), stop=(ic == 15),
                               ins=[lnT_b, wpool_b], outs=[pbig_b], psafe=True)
                    ctx.op("tensor", "matmul", pg[:, 0:384], st_ap,
                           wpool[:, ic * 1408 + 1024:(ic + 1) * 1408], start=(ic == 0), stop=(ic == 15),
                           ins=[lnT_b, wpool_b], outs=[pg_b], psafe=True)
                    for c5 in range(2):
                        ctx.op("tensor", "matmul", pbig[:, 1024 + c5 * 512:1024 + (c5 + 1) * 512], st_ap,
                               wpool[:, 22528 + ic * 1408 + c5 * 512:22528 + ic * 1408 + (c5 + 1) * 512],
                               start=(ic == 0), stop=(ic == 15),
                               ins=[lnT_b, wpool_b], outs=[pbig_b], psafe=True)
                    ctx.op("tensor", "matmul", pu[:, 0:384], st_ap,
                           wpool[:, 22528 + ic * 1408 + 1024:22528 + (ic + 1) * 1408],
                           start=(ic == 0), stop=(ic == 15),
                           ins=[lnT_b, wpool_b], outs=[pu_b], psafe=True)
                ctx.op("scalar", "activation", sg[:, 0:1024], pbig[:, 0:1024], AF.Silu,
                       ins=[pbig_b], outs=[sg_b])
                ctx.op("scalar", "activation", sg[:, 1024:1408], pg[:, 0:384], AF.Silu,
                       ins=[pg_b], outs=[sg_b])
                ctx.op("vector", "tensor_mul", sg[:, 0:1024], sg[:, 0:1024], pbig[:, 1024:2048],
                       ins=[sg_b, pbig_b], outs=[sg_b])
                ctx.op("vector", "tensor_mul", sg[:, 1024:1408], sg[:, 1024:1408], pu[:, 0:384],
                       ins=[sg_b, pu_b], outs=[sg_b])
                ctx.op("vector", "tensor_copy", scw[:, 8:9], comb[:, tl * 8 + e:tl * 8 + e + 1],
                       ins=[comb_b], outs=[scw_b])
                ctx.op("vector", "tensor_scalar_mul", sg[:, 0:1408], sg[:, 0:1408],
                       scw[:, 8:9], ins=[sg_b, scw_b], outs=[sg_b])
                ctx.op("scalar", "copy", lnb[:, 0:1408], sg[:, 0:1408], ins=[sg_b], outs=[lnb_b])
                for j in range(11):
                    ctx.op("tensor", "transpose", tp[:], lnb[:, j * 128:(j + 1) * 128], idb[:],
                           ins=[lnb_b, idb_b], outs=[tp_b], psafe=True)
                    ctx.op("scalar", "copy",
                           hpool[:, j * 512 + tl * 128:j * 512 + (tl + 1) * 128], tp[:],
                           ins=[tp_b], outs=[hpool_b])
            for ob in range(11):
                ctx.op("sync", "dma_start", out=stage[:],
                       in_=w_down[e, ob * 128:(ob + 1) * 128, :], ins=[B["w_down"]], outs=[stage_b])
                ctx.op("scalar", "copy", wpool[:, ob * 2048:(ob + 1) * 2048], stage[:],
                       ins=[stage_b], outs=[wpool_b])
            for tl in range(4):
                for ob in range(11):
                    for ch in range(4):
                        ctx.op("tensor", "matmul", pbig[:, ch * 512:(ch + 1) * 512],
                               hpool[:, ob * 512 + tl * 128:ob * 512 + (tl + 1) * 128],
                               wpool[:, ob * 2048 + ch * 512:ob * 2048 + (ch + 1) * 512],
                               start=(ob == 0), stop=(ob == 10),
                               ins=[hpool_b, wpool_b], outs=[pbig_b], psafe=True)
                ctx.op("vector", "tensor_add", acc[:, tl, :], acc[:, tl, :], pbig[:],
                       ins=[acc_b, pbig_b], outs=[acc_b])
        for tl in range(4):
            ti = grp * 4 + tl
            ctx.op("gpsimd", "dma_start", out=x2[ti * 128:(ti + 1) * 128, :], in_=acc[:, tl, :],
                   ins=[acc_b], outs=[B["x2"]])
    ctx.finish([B["x2"]])
    return nc



def build_s3(nc):
    ctx = Ctx(nc)
    x2r = dram(nc, "x2r", [2048, 2048], F32, "ExternalInput")
    wq = dram(nc, "wq", [2048, 2048], F32, "ExternalInput")
    wk = dram(nc, "wk", [2048, 2048], F32, "ExternalInput")
    wv = dram(nc, "wv", [2048, 2048], F32, "ExternalInput")
    bq_t = dram(nc, "bq_t", [128, 2048], F32, "ExternalInput")
    bk_t = dram(nc, "bk_t", [128, 2048], F32, "ExternalInput")
    bv_t = dram(nc, "bv_t", [128, 2048], F32, "ExternalInput")
    gq_t = dram(nc, "gq_t", [128, 2048], F32, "ExternalInput")
    gk_t = dram(nc, "gk_t", [128, 2048], F32, "ExternalInput")
    wmix_t = dram(nc, "wmix_t", [128, 5], F32, "ExternalInput")
    idb_d = dram(nc, "idb", [128, 128], BF16, "ExternalInput")
    ao = dram(nc, "ao", [1024, 2048], F32, "ExternalOutput")
    hsT_s = dram(nc, "hsT_s", [128, 16, 2048], BF16, "Internal")
    kT_s = dram(nc, "kT1_s", [16, 128, 2048], BF16, "Internal")
    vT_s = dram(nc, "vT1_s", [16, 128, 2048], BF16, "Internal")
    qm_s = dram(nc, "qm1_s", [1024, 2048], BF16, "Internal")
    B = {n: Buf() for n in ("x2r", "wq", "wk", "wv", "bq_t", "bk_t", "bv_t", "gq_t", "gk_t",
                            "wmix_t", "idb", "ao", "hsT_s", "kT_s", "vT_s", "qm_s")}

    idb, idb_b = sbuf(ctx, "idb_s", [128, 128], BF16)
    ctx.op("sync", "dma_start", out=idb[:], in_=idb_d[:, :], outs=[idb_b])
    wmix, wmix_b = sbuf(ctx, "wmix", [128, 5], F32)
    ctx.op("sync", "dma_start", out=wmix[:], in_=wmix_t[:, :], ins=[B["wmix_t"]], outs=[wmix_b])
    ctx.op("vector", "tensor_scalar_mul", wmix[:], wmix[:], RS, ins=[wmix_b], outs=[wmix_b])

    xt, xt_b = sbuf(ctx, "xt", [128, 2048], F32)
    ln, ln_b = sbuf(ctx, "ln", [128, 2048], F32)
    lnb, lnb_b = sbuf(ctx, "lnb", [128, 2048], BF16)
    tp, tp_b = psum(ctx, "tp", [128, 128], BF16)
    tcol, tcol_b = sbuf(ctx, "tcol", [128, 16, 128], BF16)
    tmp = mk_ln_tmp(ctx, "lnt", 2048)
    wT, wT_b = sbuf(ctx, "wT", [128, 16, 2048], BF16)
    hsblk, hsblk_b = sbuf(ctx, "hsblk", [128, 16, 128], BF16)
    pp, pp_b = psum(ctx, "pp", [128, 2048], F32)
    rows, rows_b = sbuf(ctx, "rows", [128, 2048], F32)
    qmr, qmr_b = sbuf(ctx, "qmr", [128, 2048], F32)
    gbuf, gbuf_b = sbuf(ctx, "gbuf", [128, 2048], F32)
    bbuf, bbuf_b = sbuf(ctx, "bbuf", [128, 2048], F32)
    eps6, eps6_b = sbuf(ctx, "eps6", [128, 1], F32)
    ctx.op("vector", "memset", eps6[:], 1e-6, outs=[eps6_b])

    for ti in range(16):
        ctx.op("sync", "dma_start", out=xt[:], in_=x2r[ti * 128:(ti + 1) * 128, :],
               ins=[B["x2r"]], outs=[xt_b])
        emit_layernorm(ctx, ln[:], ln_b, xt[:], xt_b, tmp, 2048)
        ctx.op("scalar", "copy", lnb[:], ln[:], ins=[ln_b], outs=[lnb_b])
        emit_tr16(ctx, lnb[:], lnb_b, idb, idb_b, tp, tp_b, tcol, tcol_b)
        ctx.op("gpsimd", "dma_start", out=hsT_s[:, :, ti * 128:(ti + 1) * 128], in_=tcol[:],
               ins=[tcol_b], outs=[B["hsT_s"]])

    for mat, wd, wdb, bt, btb, gt, gtb in (
            ("v", wv, B["wv"], bv_t, B["bv_t"], None, None),
            ("k", wk, B["wk"], bk_t, B["bk_t"], gk_t, B["gk_t"]),
            ("q", wq, B["wq"], bq_t, B["bq_t"], gq_t, B["gq_t"])):
        emit_build_wT(ctx, wT, wT_b, wd, wdb, 16, xt[:], xt_b, lnb[:], lnb_b,
                      idb, idb_b, tp, tp_b, tcol, tcol_b)
        ctx.op("sync", "dma_start", out=bbuf[:], in_=bt[:, :], ins=[btb], outs=[bbuf_b])
        if gt is not None:
            ctx.op("sync", "dma_start", out=gbuf[:], in_=gt[:, :], ins=[gtb], outs=[gbuf_b])
        tis = range(8, 16) if mat == "q" else range(16)
        for ti in tis:
            ctx.op("sync", "dma_start", out=hsblk[:], in_=hsT_s[:, :, ti * 128:(ti + 1) * 128],
                   ins=[B["hsT_s"]], outs=[hsblk_b])
            for ch in range(4):
                for ic in range(16):
                    ctx.op("tensor", "matmul", pp[:, ch * 512:(ch + 1) * 512],
                           hsblk[:, ic, :], wT[:, ic, ch * 512:(ch + 1) * 512],
                           start=(ic == 0), stop=(ic == 15),
                           ins=[hsblk_b, wT_b], outs=[pp_b], psafe=True)
            ctx.op("vector", "tensor_add", rows[:], pp[:], bbuf[:], ins=[pp_b, bbuf_b], outs=[rows_b])
            if mat in ("q", "k"):
                ctx.op("vector", "tensor_mul", ln[:], rows[:], rows[:], ins=[rows_b], outs=[ln_b])
                ctx.op("vector", "reduce_sum", tmp["var"][0][:, 0:1], ln[:], axis=AX.X,
                       ins=[ln_b], outs=[tmp["var"][1]])
                ctx.op("scalar", "mul", tmp["var"][0][:, 3:4], tmp["var"][0][:, 0:1], 1.0 / 2048,
                       ins=[tmp["var"][1]], outs=[tmp["var"][1]])
                ctx.op("scalar", "activation", tmp["var"][0][:, 1:2], tmp["var"][0][:, 3:4],
                       AF.Ln, bias=eps6[:, 0:1], ins=[tmp["var"][1], eps6_b], outs=[tmp["var"][1]])
                ctx.op("scalar", "activation", tmp["var"][0][:, 2:3], tmp["var"][0][:, 1:2],
                       AF.Exp, scale=-0.5, ins=[tmp["var"][1]], outs=[tmp["var"][1]])
                ctx.op("vector", "tensor_scalar_mul", rows[:], rows[:], tmp["var"][0][:, 2:3],
                       ins=[rows_b, tmp["var"][1]], outs=[rows_b])
                ctx.op("vector", "tensor_mul", rows[:], rows[:], gbuf[:],
                       ins=[rows_b, gbuf_b], outs=[rows_b])
            if mat == "q":
                ctx.op("vector", "tensor_scalar_mul", qmr[:], rows[:], wmix[:, 0:1],
                       ins=[rows_b, wmix_b], outs=[qmr_b])
                for h in range(16):
                    o = h * 128
                    for idx, c in enumerate((1, 2, 4, 8)):
                        sc = wmix[:, idx + 1:idx + 2]
                        ctx.op("vector", "scalar_tensor_tensor", qmr[:, o + c:o + 128],
                               rows[:, o:o + 128 - c], sc, qmr[:, o + c:o + 128],
                               ALU.mult, ALU.add, ins=[rows_b, wmix_b, qmr_b], outs=[qmr_b])
                        ctx.op("vector", "scalar_tensor_tensor", qmr[:, o:o + c],
                               rows[:, o + 128 - c:o + 128], sc, qmr[:, o:o + c],
                               ALU.mult, ALU.add, ins=[rows_b, wmix_b, qmr_b], outs=[qmr_b])
                ctx.op("scalar", "copy", lnb[:], qmr[:], ins=[qmr_b], outs=[lnb_b])
                ctx.op("gpsimd", "dma_start", out=qm_s[(ti - 8) * 128:(ti - 7) * 128, :],
                       in_=lnb[:], ins=[lnb_b], outs=[B["qm_s"]])
            else:
                dst, dstb = (kT_s, B["kT_s"]) if mat == "k" else (vT_s, B["vT_s"])
                ctx.op("scalar", "copy", lnb[:], rows[:], ins=[rows_b], outs=[lnb_b])
                for h in range(16):
                    ctx.op("tensor", "transpose", tp[:], lnb[:, h * 128:(h + 1) * 128], idb[:],
                           ins=[lnb_b, idb_b], outs=[tp_b], psafe=True)
                    ctx.op("scalar", "copy", tcol[:, 0, :], tp[:], ins=[tp_b], outs=[tcol_b])
                    ctx.op("gpsimd", "dma_start", out=dst[h, :, ti * 128:(ti + 1) * 128],
                           in_=tcol[:, 0, :], ins=[tcol_b], outs=[dstb])

    kT_ext, kT_ext_b = sbuf(ctx, "kT_ext", [128, EXTW], BF16)
    vT_ext, vT_ext_b = sbuf(ctx, "vT_ext", [128, EXTW], BF16)
    qm_h, qm_h_b = sbuf(ctx, "qm_h", [128, 8, 128], BF16)
    kst, kst_b = psum(ctx, "kst", [128, 16, 128], BF16)
    junk, junk_b = sbuf(ctx, "junk", [128, 128], BF16)
    mixs, mixs_b = sbuf(ctx, "mixs", [128, 16], F32)
    sm, sm_b = sbuf(ctx, "sm", [128, 4], F32)
    attnw, attnw_b = sbuf(ctx, "attnw", [128, 16], F32)
    oacc, oacc_b = sbuf(ctx, "oacc", [128, 128], F32)
    for h in range(16):
        for (dst, dstb, src, srcb) in ((kT_ext, kT_ext_b, kT_s, B["kT_s"]),
                                       (vT_ext, vT_ext_b, vT_s, B["vT_s"])):
            ctx.op("sync", "dma_start", out=dst[:, 0:2048], in_=src[h, :, :], ins=[srcb], outs=[dstb])
            ctx.op("sync", "dma_start", out=dst[:, 2048:EXTW], in_=src[h, :, 0:EXTW - 2048],
                   ins=[srcb], outs=[dstb])
        for lt in range(8):
            ctx.op("sync", "dma_start", out=qm_h[:, lt, :],
                   in_=qm_s[lt * 128:(lt + 1) * 128, h * 128:(h + 1) * 128],
                   ins=[B["qm_s"]], outs=[qm_h_b])
        for lt in range(8):
            base = 1024 + lt * 128
            for si, s in enumerate(SHIFTS):
                st0 = base - s
                ctx.op("tensor", "transpose", kst[:, si, :], kT_ext[:, st0:st0 + 128], idb[:],
                       ins=[kT_ext_b, idb_b], outs=[kst_b], psafe=True)
            for si in range(16):
                ctx.op("vector", "tensor_mul", junk[:], qm_h[:, lt, :], kst[:, si, :],
                       ins=[qm_h_b, kst_b], outs=[junk_b])
                ctx.op("vector", "reduce_sum", mixs[:, si:si + 1], junk[:], axis=AX.X,
                       ins=[junk_b], outs=[mixs_b])
            ctx.op("vector", "reduce_max", sm[:, 0:1], mixs[:], axis=AX.X, ins=[mixs_b], outs=[sm_b])
            ctx.op("scalar", "mul", sm[:, 1:2], sm[:, 0:1], -1.0, ins=[sm_b], outs=[sm_b])
            ctx.op("scalar", "activation", attnw[:], mixs[:], AF.Exp, bias=sm[:, 1:2],
                   ins=[mixs_b, sm_b], outs=[attnw_b])
            ctx.op("vector", "reduce_sum", sm[:, 2:3], attnw[:], axis=AX.X, ins=[attnw_b], outs=[sm_b])
            ctx.op("vector", "reciprocal", sm[:, 3:4], sm[:, 2:3], ins=[sm_b], outs=[sm_b])
            ctx.op("vector", "tensor_scalar_mul", attnw[:], attnw[:], sm[:, 3:4],
                   ins=[attnw_b, sm_b], outs=[attnw_b])
            for si, s in enumerate(SHIFTS):
                st0 = base - s
                ctx.op("tensor", "transpose", kst[:, si, :], vT_ext[:, st0:st0 + 128], idb[:],
                       ins=[vT_ext_b, idb_b], outs=[kst_b], psafe=True)
            for si in range(16):
                if si == 0:
                    ctx.op("vector", "tensor_scalar_mul", oacc[:], kst[:, 0, :], attnw[:, 0:1],
                           ins=[kst_b, attnw_b], outs=[oacc_b])
                else:
                    ctx.op("vector", "scalar_tensor_tensor", oacc[:], kst[:, si, :],
                           attnw[:, si:si + 1], oacc[:], ALU.mult, ALU.add,
                           ins=[kst_b, attnw_b, oacc_b], outs=[oacc_b])
            ctx.op("gpsimd", "dma_start", out=ao[lt * 128:(lt + 1) * 128, h * 128:(h + 1) * 128],
                   in_=oacc[:], ins=[oacc_b], outs=[B["ao"]])
    ctx.finish([B["ao"]])
    return nc



def build_s4(nc):
    ctx = Ctx(nc)
    x2 = dram(nc, "x2", [1024, 2048], F32, "ExternalInput")
    ao = dram(nc, "ao", [1024, 2048], F32, "ExternalInput")
    wo = dram(nc, "wo", [2048, 2048], F32, "ExternalInput")
    bo_t = dram(nc, "bo_t", [128, 2048], F32, "ExternalInput")
    lw = dram(nc, "lw", [2048, 2048], F32, "ExternalInput")
    lb_t = dram(nc, "lb_t", [128, 2048], F32, "ExternalInput")
    idb_d = dram(nc, "idb", [128, 128], BF16, "ExternalInput")
    out = dram(nc, "out", [1024, 2048], F32, "ExternalOutput")
    x3_s = dram(nc, "x3_s", [1024, 2048], F32, "Internal")
    B = {n: Buf() for n in ("x2", "ao", "wo", "bo_t", "lw", "lb_t", "idb", "out", "x3_s")}

    idb, idb_b = sbuf(ctx, "idb_s", [128, 128], BF16)
    ctx.op("sync", "dma_start", out=idb[:], in_=idb_d[:, :], outs=[idb_b])
    bo, bo_b = sbuf(ctx, "bo", [128, 2048], F32)
    ctx.op("sync", "dma_start", out=bo[:], in_=bo_t[:, :], ins=[B["bo_t"]], outs=[bo_b])
    xt, xt_b = sbuf(ctx, "xt", [128, 2048], F32)
    lnb, lnb_b = sbuf(ctx, "lnb", [128, 2048], BF16)
    tp, tp_b = psum(ctx, "tp", [128, 128], BF16)
    tcol, tcol_b = sbuf(ctx, "tcol", [128, 16, 128], BF16)
    wT, wT_b = sbuf(ctx, "wT", [128, 16, 2048], BF16)
    oT, oT_b = sbuf(ctx, "oT", [128, 16, 1024], BF16)
    pp, pp_b = psum(ctx, "pp", [128, 2048], F32)
    fm, fm_b = sbuf(ctx, "fm", [128, 2048], F32)
    ln, ln_b = sbuf(ctx, "ln", [128, 2048], F32)
    tmp = mk_ln_tmp(ctx, "lnt", 2048)

    emit_build_wT(ctx, wT, wT_b, wo, B["wo"], 16, xt[:], xt_b, lnb[:], lnb_b,
                  idb, idb_b, tp, tp_b, tcol, tcol_b)
    for ti in range(8):
        ctx.op("sync", "dma_start", out=xt[:], in_=ao[ti * 128:(ti + 1) * 128, :],
               ins=[B["ao"]], outs=[xt_b])
        ctx.op("scalar", "copy", lnb[:], xt[:], ins=[xt_b], outs=[lnb_b])
        emit_tr16(ctx, lnb[:], lnb_b, idb, idb_b, tp, tp_b, tcol, tcol_b)
        for ic in range(16):
            ctx.op("vector", "tensor_copy", oT[:, ic, ti * 128:(ti + 1) * 128], tcol[:, ic, :],
                   ins=[tcol_b], outs=[oT_b])
    for ti in range(8):
        for ch in range(4):
            for ic in range(16):
                ctx.op("tensor", "matmul", pp[:, ch * 512:(ch + 1) * 512],
                       oT[:, ic, ti * 128:(ti + 1) * 128], wT[:, ic, ch * 512:(ch + 1) * 512],
                       start=(ic == 0), stop=(ic == 15),
                       ins=[oT_b, wT_b], outs=[pp_b], psafe=True)
        ctx.op("sync", "dma_start", out=xt[:], in_=x2[ti * 128:(ti + 1) * 128, :],
               ins=[B["x2"]], outs=[xt_b])
        ctx.op("vector", "tensor_add", fm[:], pp[:], xt[:], ins=[pp_b, xt_b], outs=[fm_b])
        ctx.op("vector", "tensor_add", fm[:], fm[:], bo[:], ins=[fm_b, bo_b], outs=[fm_b])
        ctx.op("gpsimd", "dma_start", out=x3_s[ti * 128:(ti + 1) * 128, :], in_=fm[:],
               ins=[fm_b], outs=[B["x3_s"]])

    al, al_b = sbuf(ctx, "al", [128, 4], F32)
    sg, sg_b = sbuf(ctx, "sg", [128, 2048], F32)
    for ob in range(16):
        ctx.op("sync", "dma_start", out=xt[:], in_=lw[ob * 128:(ob + 1) * 128, :],
               ins=[B["lw"]], outs=[xt_b])
        ctx.op("vector", "reduce_sum", al[:, 0:1], xt[:], axis=AX.X, apply_absolute_value=True,
               ins=[xt_b], outs=[al_b])
        ctx.op("scalar", "mul", al[:, 1:2], al[:, 0:1], 1.0 / 2048, ins=[al_b], outs=[al_b])
        ctx.op("vector", "tensor_scalar", sg[:], xt[:], 0.0, None, ALU.is_ge,
               ins=[xt_b], outs=[sg_b])
        ctx.op("vector", "tensor_scalar", sg[:], sg[:], 2.0, -1.0, ALU.mult, ALU.add,
               ins=[sg_b], outs=[sg_b])
        ctx.op("vector", "tensor_scalar_mul", sg[:], sg[:], al[:, 1:2], ins=[sg_b, al_b], outs=[sg_b])
        ctx.op("scalar", "copy", lnb[:], sg[:], ins=[sg_b], outs=[lnb_b])
        emit_tr16(ctx, lnb[:], lnb_b, idb, idb_b, tp, tp_b, tcol, tcol_b)
        for ic in range(16):
            ctx.op("vector", "tensor_copy", wT[:, ic, ob * 128:(ob + 1) * 128], tcol[:, ic, :],
                   ins=[tcol_b], outs=[wT_b])
    lb, lb_b = sbuf(ctx, "lb", [128, 2048], F32)
    ctx.op("sync", "dma_start", out=lb[:], in_=lb_t[:, :], ins=[B["lb_t"]], outs=[lb_b])
    for ti in range(8):
        ctx.op("sync", "dma_start", out=xt[:], in_=x3_s[ti * 128:(ti + 1) * 128, :],
               ins=[B["x3_s"]], outs=[xt_b])
        emit_layernorm(ctx, ln[:], ln_b, xt[:], xt_b, tmp, 2048)
        ctx.op("scalar", "copy", lnb[:], ln[:], ins=[ln_b], outs=[lnb_b])
        emit_tr16(ctx, lnb[:], lnb_b, idb, idb_b, tp, tp_b, tcol, tcol_b)
        for ch in range(4):
            for ic in range(16):
                ctx.op("tensor", "matmul", pp[:, ch * 512:(ch + 1) * 512],
                       tcol[:, ic, :], wT[:, ic, ch * 512:(ch + 1) * 512],
                       start=(ic == 0), stop=(ic == 15),
                       ins=[tcol_b, wT_b], outs=[pp_b], psafe=True)
        ctx.op("vector", "tensor_add", fm[:], pp[:], xt[:], ins=[pp_b, xt_b], outs=[fm_b])
        ctx.op("vector", "tensor_add", fm[:], fm[:], lb[:], ins=[fm_b, lb_b], outs=[fm_b])
        ctx.op("gpsimd", "dma_start", out=out[ti * 128:(ti + 1) * 128, :], in_=fm[:],
               ins=[fm_b], outs=[B["out"]])
    ctx.finish([B["out"]])
    return nc



def _f(a):
    return np.ascontiguousarray(np.asarray(a), dtype=np.float32)


def _tile(v):
    return np.ascontiguousarray(np.tile(_f(v)[None, :], (128, 1)))


def kernel(**inp):
    x = _f(inp["L0_attn_hidden_states"])
    idb = np.ascontiguousarray(np.eye(128, dtype=ml_dtypes.bfloat16))
    trilT = np.ascontiguousarray(np.triu(np.ones((128, 128), np.float32)).astype(ml_dtypes.bfloat16))
    gamma_t = _tile(inp["L0_attn_gamma"])
    beta_t = _tile(inp["L0_attn_beta"])
    gate_w = np.ascontiguousarray(_f(inp["L0_ffn_gate_w"]).reshape(16, 128, 8).transpose(1, 0, 2))
    cores = list(range(8))

    Wq, Wk, Wv = _f(inp["L0_attn_Wq"]), _f(inp["L0_attn_Wk"]), _f(inp["L0_attn_Wv"])

    nc1 = build_s1a(bass.Bass("TRN2", target_bir_lowering=False))
    maps = []
    for c in cores:
        b, hh = c // 2, c % 2
        maps.append({"xb": x[b],
                     "Wqh": np.ascontiguousarray(Wq[hh * 1024:(hh + 1) * 1024]),
                     "Wkh": np.ascontiguousarray(Wk[hh * 1024:(hh + 1) * 1024]),
                     "Wvh": np.ascontiguousarray(Wv[hh * 1024:(hh + 1) * 1024]),
                     "gamma_t": gamma_t, "beta_t": beta_t, "idb": idb, "trilT": trilT})
    res = run_bass_kernel_spmd(nc1, maps, core_ids=cores)
    o = np.empty((4, 2048, 2048), np.float32)
    for c in cores:
        b, hh = c // 2, c % 2
        o[b, :, hh * 1024:(hh + 1) * 1024] = res.results[c]["o_out"]

    o_rows = o.reshape(8, 1024, 2048)
    x_rows = x.reshape(8, 1024, 2048)
    nc2 = build_s1b(bass.Bass("TRN2", target_bir_lowering=False))
    maps = [{"orows": np.ascontiguousarray(o_rows[c]), "xown": np.ascontiguousarray(x_rows[c]),
             "Wo": _f(inp["L0_attn_Wo"]), "idb": idb} for c in cores]
    res = run_bass_kernel_spmd(nc2, maps, core_ids=cores)
    x1 = np.stack([res.results[c]["x1"] for c in cores])

    nc3 = build_s2(bass.Bass("TRN2", target_bir_lowering=False))
    maps = [{"x1": x1[c], "gate_w": gate_w,
             "w_gate": _f(inp["L0_ffn_w_gate"]), "w_up": _f(inp["L0_ffn_w_up"]),
             "w_down": _f(inp["L0_ffn_w_down"]), "sw_gate": _f(inp["L0_ffn_sw_gate"]),
             "sw_up": _f(inp["L0_ffn_sw_up"]), "sw_down": _f(inp["L0_ffn_sw_down"]),
             "idb": idb} for c in cores]
    res = run_bass_kernel_spmd(nc3, maps, core_ids=cores)
    x2 = np.stack([res.results[c]["x2"] for c in cores])

    x2_full = x2.reshape(4, 2048, 2048)
    nc4 = build_s3(bass.Bass("TRN2", target_bir_lowering=False))
    maps = []
    for c in cores:
        b, hh = c // 2, c % 2
        own0 = hh * 1024
        x2r = np.roll(x2_full[b], 1024 - own0, axis=0)
        maps.append({"x2r": np.ascontiguousarray(x2r),
                     "wq": _f(inp["L1_attn_wq"]), "wk": _f(inp["L1_attn_wk"]),
                     "wv": _f(inp["L1_attn_wv"]),
                     "bq_t": _tile(inp["L1_attn_bq"]), "bk_t": _tile(inp["L1_attn_bk"]),
                     "bv_t": _tile(inp["L1_attn_bv"]),
                     "gq_t": _tile(inp["L1_attn_gq"]), "gk_t": _tile(inp["L1_attn_gk"]),
                     "wmix_t": np.ascontiguousarray(np.tile(_f(inp["L1_attn_w_mix"]).reshape(1, 5), (128, 1))),
                     "idb": idb})
    res = run_bass_kernel_spmd(nc4, maps, core_ids=cores)
    ao = np.stack([res.results[c]["ao"] for c in cores])

    nc5 = build_s4(bass.Bass("TRN2", target_bir_lowering=False))
    maps = [{"x2": x2[c], "ao": ao[c], "wo": _f(inp["L1_attn_wo"]), "bo_t": _tile(inp["L1_attn_bo"]),
             "lw": _f(inp["L1_ffn_weight"]), "lb_t": _tile(inp["L1_ffn_bias"]),
             "idb": idb} for c in cores]
    res = run_bass_kernel_spmd(nc5, maps, core_ids=cores)
    out = np.stack([res.results[c]["out"] for c in cores])
    return np.ascontiguousarray(out.reshape(4, 2048, 2048))
